# version 02
# baseline (speedup 1.0000x reference)
import numpy as np
import ml_dtypes
from contextlib import ExitStack

import concourse.bass as bass
import concourse.mybir as mybir
from concourse.bass_utils import run_bass_kernel_spmd

F32 = mybir.dt.float32
BF16 = mybir.dt.bfloat16
I32 = mybir.dt.int32
AF = mybir.ActivationFunctionType
ALU = mybir.AluOpType
AX = mybir.AxisListType

B, S, D = 4, 2048, 2048
NCORE = 8
TC = 1024
NH, DK, DV = 4, 256, 512
CHUNK = 64
LNEPS = 1e-5
RMSEPS = 1e-5
F32EPS = 1.1920928955078125e-07
MAGIC = 12582912.0

bfdt = ml_dtypes.bfloat16



class Sem:
    def __init__(self, h):
        self.h = h
        self.n = 0

    def inc(self, instr, k=1):
        instr.then_inc(self.h, k)
        self.n += k
        return self.n


class Prog:
    def __init__(self, name="prog"):
        self.nc = bass.Bass("TRN2", target_bir_lowering=False,
                            detect_race_conditions=False)
        self.ctx = ExitStack()
        self.name = name
        self._semid = 0

    def make_ctrs(self):
        self.vc = self.sem("vc")
        self.sc = self.sem("sc")
        self.tc = self.sem("tc")
        self.pc = self.sem("pc")

    def sem(self, name=None):
        self._semid += 1
        h = self.ctx.enter_context(
            self.nc.semaphore(name or f"s{self._semid}"))
        return Sem(h)

    def sb(self, name, shape, dtype):
        return self.ctx.enter_context(self.nc.sbuf_tensor(name, shape, dtype))

    def ps(self, name, shape, dtype=F32):
        return self.ctx.enter_context(self.nc.psum_tensor(name, shape, dtype))

    def din(self, name, shape, dtype=F32):
        return self.nc.dram_tensor(name, list(shape), dtype,
                                   kind="ExternalInput").ap()

    def dout(self, name, shape, dtype=F32):
        return self.nc.dram_tensor(name, list(shape), dtype,
                                   kind="ExternalOutput").ap()

    def start(self):
        self.block = self.ctx.enter_context(self.nc.Block())
        return self.nc

    def finish(self):
        self.ctx.close()
        return self.nc


class Bank:

    def __init__(self, p, i, shape=(128, 512), dtype=F32, name=None, ctx=None):
        self.p = p
        if ctx is None:
            self.ap = p.ps(name or f"bank{i}", list(shape), dtype)
        else:
            self.ap = ctx.enter_context(
                p.nc.psum_tensor(name or f"bank{i}", list(shape), dtype))
        self.prod = None
        self.rel = None

    def acquire(self, tensor):
        if self.rel is not None:
            tensor.wait_ge(*self.rel)

    def produced(self, p, instr):
        n = p.tc.inc(instr)
        self.prod = (p.tc.h, n)

    def consume_wait(self, eng):
        eng.wait_ge(*self.prod)

    def release_tok(self, t):
        self.rel = t


class BankPool:
    def __init__(self, p, n, prefix="", ctx=None):
        self.banks = [Bank(p, i, name=f"{prefix}bank{i}", ctx=ctx) for i in range(n)]
        self.i = 0

    def next(self):
        b = self.banks[self.i % len(self.banks)]
        self.i += 1
        return b


def vstep(p, instr):
    n = p.vc.inc(instr)
    p.nc.vector.wait_ge(p.vc.h, n)
    return n


def emit_rsqrt(p, out, in_, ti, tf, iters=3):
    vector = p.nc.vector
    vstep(p, vector.tensor_scalar(ti, in_.bitcast(I32), 1, None,
                                  ALU.arith_shift_right))
    vstep(p, vector.tensor_scalar(ti, ti, -1, 0x5F3759DF, ALU.mult, ALU.add))
    vstep(p, vector.tensor_copy(out, ti.bitcast(F32)))
    for _ in range(iters):
        vstep(p, vector.tensor_mul(tf, out, out))
        vstep(p, vector.tensor_mul(tf, tf, in_))
        vstep(p, vector.tensor_scalar(tf, tf, -0.5, 1.5, ALU.mult, ALU.add))
        vstep(p, vector.tensor_mul(out, out, tf))


def emit_recip(p, out, in_, ti, tf, iters=3):
    emit_rsqrt(p, out, in_, ti, tf, iters)
    vstep(p, p.nc.vector.tensor_mul(out, out, out))


def load_weight(nc, sem, dst, src_ap, kt_total, chunk=4, queue="gpsimd"):
    eng = getattr(nc, queue)
    src3 = src_ap.rearrange("(t p) n -> p t n", p=128)
    for k0 in range(0, kt_total, chunk):
        k1 = min(k0 + chunk, kt_total)
        sem.inc(eng.dma_start(out=dst[:, k0:k1, :], in_=src3[:, k0:k1, :]), 16)


def np_bf16(x):
    return np.ascontiguousarray(x).astype(bfdt)


def cc(x, dt=np.float32):
    return np.ascontiguousarray(x, dtype=dt)


IDENT = np.eye(128, dtype=np.float32)



def build_l0a(phases=("ln", "qk", "gk", "vg")):
    p = Prog("l0a")
    nc = p.nc
    x_in = p.din("x", [TC, D])
    wq_in = p.din("wq", [D, 1024])
    wk_in = p.din("wk", [D, 1024])
    wv_in = p.din("wv", [D, 2048])
    wg_in = p.din("wg", [D, 2048])
    wgk1_in = p.din("wgk1", [D, 16])
    wgk2_in = p.din("wgk2", [16, 1024])
    bgk2_in = p.din("bgk2", [128, 8])
    ident_in = p.din("ident", [128, 128])

    qT_out = p.dout("qT", [8, 128, TC])
    kT_out = p.dout("kT", [8, 128, TC])
    gkls_out = p.dout("gkls", [8, 128, TC])
    v_out = p.dout("v", [TC, 2048])
    g_out = p.dout("g", [TC, 2048])
    xnT_out = p.dout("xnT", [16, 128, TC], BF16)

    xt = [p.sb(f"xt{i}", [128, D], F32) for i in range(2)]
    xn = [p.sb(f"xn{i}", [128, D], BF16) for i in range(2)]
    xnT = p.sb("s_xnT", [128, 16, TC], BF16)
    wbufA = p.sb("wbufA", [128, 16, 2048], BF16)
    wbufB = p.sb("wbufB", [128, 16, 2048], BF16)
    wgk1b = p.sb("wgk1b", [128, 16, 16], BF16)
    wgk2b = p.sb("wgk2b", [16, 1024], BF16)
    bgk2b = p.sb("bgk2b", [128, 8], F32)
    identb = p.sb("identb", [128, 128], BF16)
    stats = p.sb("stats", [128, 1], F32)
    sqs = p.sb("sqs", [128, 1], F32)
    vcol = p.sb("vcol", [128, 1], F32)
    rstd = p.sb("rstd", [128, 1], F32)
    mcol = p.sb("mcol", [128, 1], F32)
    ti32 = p.sb("ti32", [128, 1], I32)
    tf32 = p.sb("tf32", [128, 1], F32)
    scratch = p.sb("scratch", [128, D], F32)
    lnscr = scratch[:, :]
    gkpre = scratch[:, 0:512]
    gka = scratch[:, 512:1024]
    gkb = scratch[:, 1024:1536]
    evb = [p.sb(f"evb{i}", [128, 512], F32) for i in range(3)]
    tmp16 = p.sb("tmp16", [128, 4, 16], BF16)
    tmp16T = p.sb("tmp16T", [16, 512], BF16)

    nc = p.start()
    p.make_ctrs()
    vc, sc, tc = p.vc, p.sc, p.tc
    pool = BankPool(p, 6)
    trbank = Bank(p, 6, shape=(128, 1024), dtype=BF16, name="trps")
    tsbank = Bank(p, 7, shape=(16, 512), dtype=BF16, name="trps_s")

    ld = p.sem("ld")
    gld = p.sem("gld")
    st = p.sem("st")

    gld.inc(nc.gpsimd.dma_start(out=identb[:, :], in_=ident_in), 16)
    gld_ident = gld.n
    load_weight(nc, gld, wbufA[:, :, 0:1024], wq_in, 16)
    gld_wq = gld.n
    load_weight(nc, gld, wbufB[:, :, 0:1024], wk_in, 16)
    gld_wk = gld.n
    gld.inc(nc.gpsimd.dma_start(out=wgk1b[:, :, :],
                                in_=wgk1_in.rearrange("(t p) n -> p t n", p=128)), 16)
    gld.inc(nc.gpsimd.dma_start(out=wgk2b[:, :], in_=wgk2_in), 16)
    gld_gk = gld.n
    ld.inc(nc.sync.dma_start(out=bgk2b[:, :], in_=bgk2_in), 16)

    ln_done_v = [None] * 8
    ln_done_s = [None] * 8
    tr_done = {}
    xnT_tok = [None] * 16
    for it in range(8):
        bi = it % 2
        if it >= 2:
            nc.sync.wait_ge(*ln_done_v[it - 2])
            nc.sync.wait_ge(*ln_done_s[it - 2])
        ld.inc(nc.sync.dma_start(out=xt[bi][:, :],
                                 in_=x_in[128 * it:128 * (it + 1), :]), 16)
        nc.vector.wait_ge(ld.h, ld.n)
        vstep(p, nc.vector.reduce_sum(stats[:, :], xt[bi][:, :], AX.X))
        nc.scalar.wait_ge(ld.h, ld.n)
        si = nc.scalar.activation(lnscr, xt[bi][:, :], AF.Square,
                                  accum_out=sqs[:, :])
        ln_done_s[it] = (sc.h, sc.inc(si))
        nc.vector.wait_ge(*ln_done_s[it])
        vstep(p, nc.vector.tensor_scalar(mcol[:, :], stats[:, :], 1.0 / D, None, ALU.mult))
        vstep(p, nc.vector.tensor_mul(tf32[:, :], mcol[:, :], mcol[:, :]))
        vstep(p, nc.vector.tensor_scalar(vcol[:, :], sqs[:, :], 1.0 / D, None, ALU.mult))
        vstep(p, nc.vector.tensor_sub(vcol[:, :], vcol[:, :], tf32[:, :]))
        vstep(p, nc.vector.tensor_scalar(vcol[:, :], vcol[:, :], LNEPS, None, ALU.add))
        emit_rsqrt(p, rstd[:, :], vcol[:, :], ti32[:, :], tf32[:, :])
        if it >= 2:
            nc.vector.wait_ge(*tr_done[(it - 2, 1)])
        wi = nc.vector.tensor_scalar(xn[bi][:, :], xt[bi][:, :],
                                     mcol[:, :], rstd[:, :],
                                     ALU.subtract, ALU.mult)
        ln_done_v[it] = (vc.h, vc.inc(wi))
        for half in range(2):
            nc.tensor.wait_ge(gld.h, gld_ident)
            nc.tensor.wait_ge(*ln_done_v[it])
            trbank.acquire(nc.tensor)
            last = None
            for j in range(8):
                dt_i = 8 * half + j
                last = nc.tensor.transpose(trbank.ap[:, 128 * j:128 * (j + 1)],
                                           xn[bi][:, 128 * dt_i:128 * (dt_i + 1)],
                                           identb[:, :])
            trbank.produced(p, last)
            tr_done[(it, half)] = trbank.prod
            nc.vector.wait_ge(*trbank.prod)
            cp = nc.vector.tensor_copy(
                xnT[:, 8 * half:8 * half + 8, 128 * it:128 * (it + 1)],
                trbank.ap[:, :].rearrange("p (t n) -> p t n", t=8))
            t = (vc.h, vc.inc(cp))
            trbank.release_tok(t)
            xnT_tok[2 * it + half] = t
    xnT_ready = xnT_tok[-1]

    if "xnst" in phases or "qk" in phases:
        nc.scalar.wait_ge(*xnT_ready)
        xo = xnT_out.rearrange("t p n -> p t n")
        for k0 in range(0, 16, 4):
            st.inc(nc.scalar.dma_start(out=xo[:, k0:k0 + 4, :],
                                       in_=xnT[:, k0:k0 + 4, :]), 16)

    if "qk" not in phases:
        nc.scalar.wait_ge(st.h, st.n)
        p.finish()
        return p.nc

    evtok = [None, None, None]
    evi = [0]

    def proj_store(dst_ap, lhs_tiles, rhs_tiles, bias_ap=None):
        bk = pool.next()
        bk.acquire(nc.tensor)
        nkt = len(lhs_tiles)
        ncols = rhs_tiles[0].shape[-1]
        last = None
        for kt in range(nkt):
            last = nc.tensor.matmul(bk.ap[:, 0:ncols], lhs_tiles[kt],
                                    rhs_tiles[kt],
                                    start=(kt == 0), stop=(kt == nkt - 1))
        bk.produced(p, last)
        j = evi[0] % 3
        evi[0] += 1
        if bias_ap is not None:
            bk.consume_wait(nc.vector)
            if evtok[j] is not None:
                nc.vector.wait_ge(st.h, evtok[j])
            cpi = nc.vector.tensor_scalar(evb[j][:, 0:ncols], bk.ap[:, 0:ncols],
                                          bias_ap, None, ALU.add)
            n = vc.inc(cpi)
            bk.release_tok((vc.h, n))
            nc.scalar.wait_ge(vc.h, n)
        else:
            bk.consume_wait(nc.scalar)
            if evtok[j] is not None:
                nc.scalar.wait_ge(st.h, evtok[j])
            cpi = nc.scalar.copy(evb[j][:, 0:ncols], bk.ap[:, 0:ncols])
            n = sc.inc(cpi)
            bk.release_tok((sc.h, n))
            nc.scalar.wait_ge(sc.h, n)
        st.inc(nc.scalar.dma_start(out=dst_ap, in_=evb[j][:, 0:ncols]), 16)
        evtok[j] = st.n
        return tc

    nc.tensor.wait_ge(*xnT_ready)
    nc.tensor.wait_ge(gld.h, gld_wk)
    wdone = {}
    for wb, out_t, key in ((wbufA, qT_out, "q"), (wbufB, kT_out, "k")):
        for g01 in range(2):
            for m in range(8):
                lhs = [wb[:, kt, 128 * m:128 * (m + 1)] for kt in range(16)]
                rhs = [xnT[:, kt, 512 * g01:512 * (g01 + 1)] for kt in range(16)]
                proj_store(out_t[m, :, 512 * g01:512 * (g01 + 1)], lhs, rhs)
        wdone[key] = (tc.h, tc.n)

    nc.gpsimd.wait_ge(*wdone["q"])
    load_weight(nc, gld, wbufA, wv_in, 16)
    gld_wv = gld.n
    nc.gpsimd.wait_ge(*wdone["k"])
    load_weight(nc, gld, wbufB, wg_in, 16)
    gld_wg = gld.n

    nc.tensor.wait_ge(gld.h, gld_gk)
    gkls_st = None
    for g01 in (range(2) if "gk" in phases else []):
        tmp_tok = None
        for tt in range(4):
            bk = pool.next()
            bk.acquire(nc.tensor)
            last = None
            for kt in range(16):
                last = nc.tensor.matmul(
                    bk.ap[:, 0:16],
                    xnT[:, kt, 128 * (4 * g01 + tt):128 * (4 * g01 + tt + 1)],
                    wgk1b[:, kt, :],
                    start=(kt == 0), stop=(kt == 15))
            bk.produced(p, last)
            bk.consume_wait(nc.vector)
            cp = nc.vector.tensor_copy(tmp16[:, tt, :], bk.ap[:, 0:16])
            n = vc.inc(cp)
            bk.release_tok((vc.h, n))
            tmp_tok = (vc.h, n)
        nc.tensor.wait_ge(*tmp_tok)
        tsbank.acquire(nc.tensor)
        last = None
        for tt in range(4):
            last = nc.tensor.transpose(tsbank.ap[0:16, 128 * tt:128 * (tt + 1)],
                                       tmp16[:, tt, :], identb[:, :])
        tsbank.produced(p, last)
        nc.vector.wait_ge(*tsbank.prod)
        cp = nc.vector.tensor_copy(tmp16T[:, :], tsbank.ap[0:16, :])
        n = vc.inc(cp)
        tsbank.release_tok((vc.h, n))
        t16_tok = (vc.h, n)
        nc.tensor.wait_ge(*t16_tok)
        for m in range(8):
            bk = pool.next()
            bk.acquire(nc.tensor)
            mm = nc.tensor.matmul(bk.ap[:, :], wgk2b[:, 128 * m:128 * (m + 1)],
                                  tmp16T[:, :], start=True, stop=True)
            bk.produced(p, mm)
            bk.consume_wait(nc.vector)
            if gkls_st is not None:
                nc.vector.wait_ge(st.h, gkls_st)
            cpi = nc.vector.tensor_scalar(gkpre, bk.ap[:, :],
                                          bgk2b[:, m:m + 1], None, ALU.add)
            n = vc.inc(cpi)
            bk.release_tok((vc.h, n))
            nc.vector.tensor_scalar(gka, gkpre, -1.0, None, ALU.mult)
            na = vc.inc(nc.vector.tensor_tensor(gka, gka, gkpre, ALU.min))
            nc.scalar.wait_ge(vc.h, na)
            ns = sc.inc(nc.scalar.activation(gkb, gka, AF.Exp))
            nc.vector.wait_ge(sc.h, ns)
            na = vc.inc(nc.vector.tensor_scalar(gkb, gkb, 1.0, None, ALU.add))
            nc.scalar.wait_ge(vc.h, na)
            ns = sc.inc(nc.scalar.activation(gka, gkb, AF.Ln))
            nc.vector.wait_ge(sc.h, ns)
            nc.vector.tensor_scalar(gkb, gkpre, 0.0, None, ALU.min)
            na = vc.inc(nc.vector.tensor_sub(gkb, gkb, gka))
            nc.scalar.wait_ge(vc.h, na)
            st.inc(nc.scalar.dma_start(
                out=gkls_out[m, :, 512 * g01:512 * (g01 + 1)],
                in_=gkb), 16)
            gkls_st = st.n

    for wb, out_t, ready_n in (((wbufA, v_out, gld_wv), (wbufB, g_out, gld_wg))
                               if "vg" in phases else ()):
        nc.tensor.wait_ge(gld.h, ready_n)
        for g01 in range(2):
            for tt in range(4):
                for nck in range(4):
                    lhs = [xnT[:, kt, 128 * (4 * g01 + tt):128 * (4 * g01 + tt + 1)]
                           for kt in range(16)]
                    rhs = [wb[:, kt, 512 * nck:512 * (nck + 1)] for kt in range(16)]
                    row0 = 128 * (4 * g01 + tt)
                    proj_store(out_t[row0:row0 + 128, 512 * nck:512 * (nck + 1)],
                               lhs, rhs)

    nc.scalar.wait_ge(st.h, st.n)
    p.finish()
    return p.nc



LN16 = float(np.log(1.0 / 16.0))

def build_gla():
    p = Prog("gla")
    nc = p.nc
    qin = [p.din(f"q{i}", [2, 128, S]) for i in range(2)]
    kin = [p.din(f"k{i}", [2, 128, S]) for i in range(2)]
    gin = [p.din(f"gk{i}", [2, 128, S]) for i in range(2)]
    vin = [p.din(f"v{i}", [S, DV]) for i in range(2)]
    mask_in = p.din("mask", [64, 64])
    ident_in = p.din("ident", [128, 128])
    consts_in = p.din("consts", [128, 4])
    oout = [p.dout(f"o{i}", [S, DV]) for i in range(2)]

    qb = [p.sb(f"qb{i}", [128, 2, S], F32) for i in range(2)]
    kb = [p.sb(f"kb{i}", [128, 2, S], F32) for i in range(2)]
    gb = [p.sb(f"gb{i}", [128, 2, S], F32) for i in range(2)]
    vb = [p.sb(f"vb{i}", [64, 32, DV], BF16) for i in range(2)]
    maskb = p.sb("maskb", [64, 64], F32)
    identb = p.sb("identb", [128, 128], BF16)
    cb = p.sb("cb", [128, 4], F32)
    ones64 = p.sb("ones64", [128, 64], F32)
    gc = p.sb("gc", [128, 2, 64], F32)
    eg = p.sb("eg", [128, 2, 64], F32)
    egl = p.sb("egl", [128, 2], F32)
    qg = p.sb("qg", [128, 2, 64], BF16)
    kg = p.sb("kg", [128, 2, 64], BF16)
    kd = p.sb("kd", [128, 2, 64], BF16)
    kdT = p.sb("kdT", [64, 2, 128], BF16)
    Am = p.sb("Am", [64, 64], BF16)
    Sst = p.sb("Sst", [128, 2, DV], F32)
    Sbf = p.sb("Sbf", [128, 2, DV], BF16)
    evo = [p.sb(f"evo{i}", [64, DV], F32) for i in range(2)]

    nc = p.start()
    p.make_ctrs()
    vc, sc, tc = p.vc, p.sc, p.tc
    bkA = Bank(p, 0, shape=(64, 64), name="bkA")
    bkO = Bank(p, 1, shape=(64, 512), name="bkO")
    bkS = [Bank(p, 2 + i, shape=(128, 512), name=f"bkS{i}") for i in range(2)]
    bkT = Bank(p, 4, shape=(64, 256), dtype=BF16, name="bkT")

    ld = p.sem("ld")
    gld = p.sem("gld")
    st = p.sem("st")

    for i in range(2):
        ld.inc(nc.sync.dma_start(out=qb[i][:, :, :],
                                 in_=qin[i].rearrange("t p n -> p t n")), 16)
        ld.inc(nc.sync.dma_start(out=kb[i][:, :, :],
                                 in_=kin[i].rearrange("t p n -> p t n")), 16)
        ld.inc(nc.sync.dma_start(out=gb[i][:, :, :],
                                 in_=gin[i].rearrange("t p n -> p t n")), 16)
        v3 = vin[i].rearrange("(c p) n -> p c n", p=64)
        for k0 in range(0, 32, 8):
            gld.inc(nc.gpsimd.dma_start(out=vb[i][:, k0:k0 + 8, :],
                                        in_=v3[:, k0:k0 + 8, :]), 16)
    ld.inc(nc.sync.dma_start(out=maskb[:, :], in_=mask_in), 16)
    gld.inc(nc.gpsimd.dma_start(out=identb[:, :], in_=ident_in), 16)
    ld.inc(nc.sync.dma_start(out=cb[:, :], in_=consts_in), 16)
    ld_all = ld.n
    gld_all = gld.n
    vstep(p, nc.vector.memset(ones64[:, :], 1.0))

    ev_tok = [None, None]
    for i in range(2):
        nc.vector.wait_ge(ld.h, ld_all)
        nc.scalar.wait_ge(ld.h, ld_all)
        nc.tensor.wait_ge(gld.h, gld_all)
        nc.tensor.wait_ge(ld.h, ld_all)
        sbf_tok = None
        sgl_prev = None
        last_t_prev = None
        for c in range(32):
            t0 = 64 * c

            if sgl_prev is not None:
                nc.vector.wait_ge(*sgl_prev)
            if last_t_prev is not None:
                nc.vector.wait_ge(*last_t_prev)
            for kt in range(2):
                vstep(p, nc.vector.tensor_tensor_scan(
                    gc[:, kt, :], ones64[:, :], gb[i][:, kt, t0:t0 + 64],
                    0.0, ALU.mult, ALU.add))
            gc_tok = (vc.h, vc.n)
            nc.scalar.wait_ge(*gc_tok)
            nc.scalar.activation(eg[:, 0, :], gc[:, 0, :], AF.Exp,
                                 bias=cb[:, 2:3], scale=cb[:, 0:1])
            s1 = sc.inc(nc.scalar.activation(eg[:, 1, :], gc[:, 1, :], AF.Exp,
                                             bias=cb[:, 2:3], scale=cb[:, 0:1]))
            nc.vector.wait_ge(sc.h, s1)
            v1 = vc.inc(nc.vector.tensor_mul(qg[:, :, :],
                                             qb[i][:, :, t0:t0 + 64], eg[:, :, :]))
            nc.scalar.wait_ge(vc.h, v1)
            nc.scalar.activation(eg[:, 0, :], gc[:, 0, :], AF.Exp, scale=cb[:, 1:2])
            s2 = sc.inc(nc.scalar.activation(eg[:, 1, :], gc[:, 1, :], AF.Exp,
                                             scale=cb[:, 1:2]))
            sgl = sc.inc(nc.scalar.activation(egl[:, :], gc[:, :, 63], AF.Exp,
                                              scale=cb[:, 0:1]))
            nc.scalar.wait_ge(sc.h, sgl)
            nc.vector.wait_ge(sc.h, s2)
            v2 = vc.inc(nc.vector.tensor_mul(kg[:, :, :],
                                             kb[i][:, :, t0:t0 + 64], eg[:, :, :]))
            nc.vector.wait_ge(vc.h, v2)
            nc.vector.wait_ge(sc.h, sgl)
            nc.vector.tensor_scalar(kd[:, 0, :], kg[:, 0, :], egl[:, 0:1],
                                    None, ALU.mult)
            v3t = vc.inc(nc.vector.tensor_scalar(kd[:, 1, :], kg[:, 1, :],
                                                 egl[:, 1:2], None, ALU.mult))
            nc.tensor.wait_ge(vc.h, v2)
            bkA.acquire(nc.tensor)
            nc.tensor.matmul(bkA.ap[:, :], kg[:, 0, :], qg[:, 0, :],
                             start=True, stop=False)
            mmA = nc.tensor.matmul(bkA.ap[:, :], kg[:, 1, :], qg[:, 1, :],
                                   start=False, stop=True)
            bkA.produced(p, mmA)
            bkA.consume_wait(nc.vector)
            cpA = nc.vector.tensor_mul(Am[:, :], bkA.ap[:, :], maskb[:, :])
            nA = vc.inc(cpA)
            bkA.release_tok((vc.h, nA))
            nc.tensor.wait_ge(vc.h, nA)
            bkO.acquire(nc.tensor)
            mmO = nc.tensor.matmul(bkO.ap[:, :], Am[:, :],
                                   vb[i][:, c, :],
                                   start=True, stop=(c == 0))
            if c > 0:
                nc.tensor.wait_ge(*sbf_tok)
                nc.tensor.matmul(bkO.ap[:, :], qg[:, 0, :], Sbf[:, 0, :],
                                 start=False, stop=False)
                mmO = nc.tensor.matmul(bkO.ap[:, :], qg[:, 1, :], Sbf[:, 1, :],
                                       start=False, stop=True)
            bkO.produced(p, mmO)
            bkO.consume_wait(nc.scalar)
            j = c % 2
            if ev_tok[j] is not None:
                nc.scalar.wait_ge(st.h, ev_tok[j])
            cpO = nc.scalar.copy(evo[j][:, :], bkO.ap[:, :])
            nO = sc.inc(cpO)
            bkO.release_tok((sc.h, nO))
            nc.scalar.wait_ge(sc.h, nO)
            st.inc(nc.scalar.dma_start(out=oout[i][t0:t0 + 64, :],
                                       in_=evo[j][:, :]), 16)
            ev_tok[j] = st.n
            nc.tensor.wait_ge(vc.h, v3t)
            bkT.acquire(nc.tensor)
            nc.tensor.transpose(bkT.ap[0:64, 0:128], kd[:, 0, :], identb[:, :])
            trT = nc.tensor.transpose(bkT.ap[0:64, 128:256], kd[:, 1, :],
                                      identb[:, :])
            bkT.produced(p, trT)
            bkT.consume_wait(nc.vector)
            cpT = nc.vector.tensor_copy(kdT[:, :, :],
                                        bkT.ap[0:64, :].rearrange("p (t n) -> p t n", t=2))
            nT = vc.inc(cpT)
            bkT.release_tok((vc.h, nT))
            nc.tensor.wait_ge(vc.h, nT)
            stok = []
            for kt in range(2):
                bkS[kt].acquire(nc.tensor)
                mmS = nc.tensor.matmul(bkS[kt].ap[:, :], kdT[:, kt, :],
                                       vb[i][:, c, :],
                                       start=True, stop=True)
                bkS[kt].produced(p, mmS)
            for kt in range(2):
                bkS[kt].consume_wait(nc.vector)
                if c == 0:
                    up = nc.vector.tensor_copy(Sst[:, kt, :], bkS[kt].ap[:, :])
                else:
                    up = nc.vector.scalar_tensor_tensor(
                        Sst[:, kt, :], Sst[:, kt, :], egl[:, kt:kt + 1],
                        bkS[kt].ap[:, :], ALU.mult, ALU.add)
                nU = vc.inc(up)
                bkS[kt].release_tok((vc.h, nU))
                nc.vector.wait_ge(vc.h, nU)
                cpS = nc.vector.tensor_copy(Sbf[:, kt, :], Sst[:, kt, :])
                vc.inc(cpS)
            sbf_tok = (vc.h, vc.n)
            sgl_prev = (sc.h, sgl)
            last_t_prev = bkS[1].prod

    nc.scalar.wait_ge(st.h, st.n)
    p.finish()
    return p.nc


def run_gla(l0a_res):
    nc = build_gla()
    consts = np.zeros((128, 4), np.float32)
    consts[:, 0] = 1.0 / 16.0
    consts[:, 1] = -1.0 / 16.0
    consts[:, 2] = LN16
    mask = np.triu(np.ones((64, 64), np.float32))
    in_maps = []
    for core in range(NCORE):
        m = dict(mask=mask, ident=IDENT, consts=consts)
        for i in range(2):
            pair = 2 * core + i
            b, h = pair // 4, pair % 4
            for nm, key in (("q", "qT"), ("k", "kT"), ("gk", "gkls")):
                arr = np.concatenate([l0a_res[2 * b][key][2 * h:2 * h + 2],
                                      l0a_res[2 * b + 1][key][2 * h:2 * h + 2]],
                                     axis=2)
                m[f"{nm}{i}"] = cc(arr)
            vfull = np.concatenate([l0a_res[2 * b]["v"], l0a_res[2 * b + 1]["v"]],
                                   axis=0)
            m[f"v{i}"] = cc(vfull[:, DV * h:DV * (h + 1)])
        in_maps.append(m)
    res = run_bass_kernel_spmd(nc, in_maps, core_ids=list(range(NCORE)))
    o = np.zeros((B, NH, S, DV), np.float32)
    for core in range(NCORE):
        for i in range(2):
            pair = 2 * core + i
            b, h = pair // 4, pair % 4
            o[b, h] = res.results[core][f"o{i}"]
    return o



def build_quant():
    p = Prog("quant")
    nc = p.nc
    wgu_in = p.din("wgu", [2048, 2048])
    wpj_in = p.din("wpj", [256, 8192])
    wu_in = p.din("wu", [256, 2048])
    wv1_in = p.din("wv1", [256, 2048])
    wg1_in = p.din("wg1", [64, 2048])
    wg2_in = p.din("wg2", [256, 512])
    wr_in = p.din("wr", [8, 2048])
    wsg_in = p.din("wsg", [1, 2048])
    ident_in = p.din("ident", [128, 128])

    o_wgu = p.dout("o_wgu", [2048, 2048], BF16)
    o_wpj = p.dout("o_wpj", [8192, 256], BF16)
    o_wu = p.dout("o_wu", [2048, 256], BF16)
    o_wv = p.dout("o_wv", [2048, 256], BF16)
    o_wg1 = p.dout("o_wg1", [2048, 64], BF16)
    o_wg2 = p.dout("o_wg2", [512, 256], BF16)
    o_wr = p.dout("o_wr", [2048, 8], BF16)
    o_wsg = p.dout("o_wsg", [2048, 1], BF16)

    wbf = p.sb("wbf", [128, 8192], BF16)
    red = p.sb("red", [128, 128], F32)
    sclf = p.sb("sclf", [128, 128], F32)
    sclb = p.sb("sclb", [128, 128], BF16)
    ti32 = p.sb("ti32", [128, 128], I32)
    tf32 = p.sb("tf32", [128, 128], F32)
    q0 = p.sb("q0", [128, 8192], BF16)
    rq = p.sb("rq", [128, 8192], F32)
    wh = p.sb("wh", [128, 8192], BF16)
    stg = [p.sb(f"stg{i}", [128, 4, 128], BF16) for i in range(2)]
    identb = p.sb("identb", [128, 128], BF16)

    nc = p.start()
    p.make_ctrs()
    vc, sc, tc = p.vc, p.sc, p.tc
    trb = Bank(p, 0, shape=(128, 512), dtype=BF16, name="trq")
    trb2 = Bank(p, 1, shape=(128, 512), dtype=BF16, name="trq2")

    ld = p.sem("ld")
    gld = p.sem("gld")
    st = p.sem("st")
    gld.inc(nc.gpsimd.dma_start(out=identb[:, :], in_=ident_in), 16)
    gld_id = gld.n
    nc.tensor.wait_ge(gld.h, gld_id)

    stg_tok = [None, None]
    stg_i = [0]
    last_tr = [None]

    def transpose_store(src_sb, rows, X, out_ap, rt, quant_done_tok):
        nxt = X // 128
        banks = [trb, trb2]
        bsz = 4 if rows == 128 else 1
        for b0 in range(0, nxt, bsz):
            bk = banks[(b0 // bsz) % 2]
            nc.tensor.wait_ge(*quant_done_tok)
            bk.acquire(nc.tensor)
            last = None
            for j in range(min(bsz, nxt - b0)):
                dt_i = b0 + j
                last = nc.tensor.transpose(
                    bk.ap[:, rows * j:rows * (j + 1)],
                    src_sb[0:rows, 128 * dt_i:128 * (dt_i + 1)],
                    identb[0:rows, 0:rows])
            bk.produced(p, last)
            last_tr[0] = bk.prod
            bk.consume_wait(nc.vector)
            js = stg_i[0] % 2
            stg_i[0] += 1
            if stg_tok[js] is not None:
                nc.vector.wait_ge(st.h, stg_tok[js])
            nj = min(bsz, nxt - b0)
            cp = nc.vector.tensor_copy(
                stg[js][:, 0:nj, 0:rows],
                bk.ap[:, 0:rows * nj].rearrange("p (t n) -> p t n", t=nj))
            n = vc.inc(cp)
            bk.release_tok((vc.h, n))
            nc.scalar.wait_ge(vc.h, n)
            dst = out_ap.rearrange("(t p) n -> p t n", p=128)[:, b0:b0 + nj,
                                                             rows * rt:rows * (rt + 1)]
            st.inc(nc.scalar.dma_start(out=dst, in_=stg[js][:, 0:nj, 0:rows],
                                       allow_slow_non_contiguous=(rows < 8)), 16)
            stg_tok[js] = st.n

    wread_tok = [None]

    def quant_shard(src_ap, R, X, out_ap):
        ngr = X // 64
        for rt in range(R // 128):
            rows = 128
            if wread_tok[0] is not None:
                nc.gpsimd.wait_ge(*wread_tok[0])
            gld.inc(nc.gpsimd.dma_start(
                out=wbf[:, 0:X], in_=src_ap[128 * rt:128 * (rt + 1), :]), 16)
            nc.vector.wait_ge(gld.h, gld.n)
            if last_tr[0] is not None:
                nc.vector.wait_ge(*last_tr[0])
            vstep(p, nc.vector.tensor_reduce(
                red[:, 0:ngr], wbf[:, 0:X].rearrange("p (g n) -> p g n", g=ngr),
                AX.X, ALU.add, apply_absolute_value=True))
            vstep(p, nc.vector.tensor_scalar(sclb[:, 0:ngr], red[:, 0:ngr],
                                             1.0 / 64, None, ALU.mult))
            vstep(p, nc.vector.tensor_copy(sclf[:, 0:ngr], sclb[:, 0:ngr]))
            vstep(p, nc.vector.tensor_scalar(sclf[:, 0:ngr], sclf[:, 0:ngr],
                                             1e-8, None, ALU.max))
            emit_recip(p, tf32[:, 0:ngr], sclf[:, 0:ngr], ti32[:, 0:ngr],
                       red[:, 0:ngr])
            recb = tf32[:, 0:ngr].unsqueeze(2).broadcast_to([128, ngr, 64])
            sclbc = sclf[:, 0:ngr].unsqueeze(2).broadcast_to([128, ngr, 64])
            w3 = wbf[:, 0:X].rearrange("p (g n) -> p g n", g=ngr)
            vstep(p, nc.vector.tensor_tensor(
                q0[:, 0:X].rearrange("p (g n) -> p g n", g=ngr), w3, recb,
                ALU.mult))
            vstep(p, nc.vector.tensor_scalar(rq[:, 0:X], q0[:, 0:X], MAGIC,
                                             MAGIC, ALU.add, ALU.subtract))
            vstep(p, nc.vector.tensor_scalar(rq[:, 0:X], rq[:, 0:X], 1.0, -1.0,
                                             ALU.min, ALU.max))
            wq_i = nc.vector.tensor_tensor(
                wh[:, 0:X].rearrange("p (g n) -> p g n", g=ngr),
                rq[:, 0:X].rearrange("p (g n) -> p g n", g=ngr), sclbc, ALU.mult)
            qtok = (vc.h, vc.inc(wq_i))
            wread_tok[0] = qtok
            transpose_store(wh, 128, X, out_ap, rt, qtok)

    def plain_shard(src_ap, R, X, out_ap):
        step = min(R, 128)
        for rt in range(max(1, R // 128)):
            if last_tr[0] is not None:
                nc.gpsimd.wait_ge(*last_tr[0])
            if wread_tok[0] is not None:
                nc.gpsimd.wait_ge(*wread_tok[0])
            gld.inc(nc.gpsimd.dma_start(
                out=wbf[0:step, 0:X], in_=src_ap[step * rt:step * (rt + 1), :]), 16)
            transpose_store(wbf, step, X, out_ap, rt, (gld.h, gld.n))

    quant_shard(wgu_in, 2048, 2048, o_wgu)
    quant_shard(wpj_in, 256, 8192, o_wpj)
    plain_shard(wu_in, 256, 2048, o_wu)
    plain_shard(wv1_in, 256, 2048, o_wv)
    plain_shard(wg1_in, 64, 2048, o_wg1)
    plain_shard(wg2_in, 256, 512, o_wg2)
    plain_shard(wr_in, 8, 2048, o_wr)
    plain_shard(wsg_in, 1, 2048, o_wsg)

    nc.scalar.wait_ge(st.h, st.n)
    p.finish()
    return p.nc


def run_quant(inputs):
    nc = build_quant()
    in_maps = []
    for c in range(NCORE):
        m = dict(
            wgu=cc(np.asarray(inputs["L0_ffn_w_gate_up"])[2048 * c:2048 * (c + 1)]),
            wpj=cc(np.asarray(inputs["L0_ffn_w_proj"])[256 * c:256 * (c + 1)]),
            wu=cc(np.asarray(inputs["L1_attn_Wu"])[256 * c:256 * (c + 1)]),
            wv1=cc(np.asarray(inputs["L1_attn_Wv"])[256 * c:256 * (c + 1)]),
            wg1=cc(np.asarray(inputs["L1_attn_Wg1"])[64 * c:64 * (c + 1)]),
            wg2=cc(np.asarray(inputs["L1_attn_Wg2"])[256 * c:256 * (c + 1)]),
            wr=cc(np.asarray(inputs["L1_ffn_W_router"])),
            wsg=cc(np.asarray(inputs["L1_ffn_W_sgate"])),
            ident=IDENT,
        )
        in_maps.append(m)
    res = run_bass_kernel_spmd(nc, in_maps, core_ids=list(range(NCORE)))
    r = res.results

    def cat(key, axis=1):
        return np.concatenate([_bfview(r[c][key]) for c in range(NCORE)], axis=axis)

    return dict(
        wguT=cat("o_wgu"), wpjT=cat("o_wpj"), wuT=cat("o_wu"), wvT=cat("o_wv"),
        wg1T=cat("o_wg1"), wg2T=cat("o_wg2"),
        wrT=_bfview(r[0]["o_wr"]), wsgT=_bfview(r[0]["o_wsg"]),
    )


def _bfview(a):
    if a.dtype == np.uint16:
        return a.view(bfdt)
    return a



def build_ffn0():
    p = Prog("ffn0")
    nc = p.nc
    x_in = p.din("x", [TC, D])
    o_in = p.din("o", [TC, D])
    g_in = p.din("g", [TC, D])
    xnT_in = p.din("xnT", [16, 128, TC], BF16)
    gnw_in = p.din("gnw", [128, 4])
    wo_in = p.din("wo", [D, D])
    wguT_in = p.din("wguT", [D, 16384], BF16)
    wpjT_in = p.din("wpjT", [8192, D], BF16)
    ident_in = p.din("ident", [128, 128])
    x1_out = p.dout("x1", [TC, D])
    s0_scr = p.nc.dram_tensor("s0scr", [TC, D], F32).ap()

    xnT = p.sb("s_xnT", [128, 16, TC], BF16)
    identb = p.sb("identb", [128, 128], BF16)
    gnwb = p.sb("gnwb", [128, 4], F32)
    rstd4 = p.sb("rstd4", [128, 4], F32)
    ti32 = p.sb("ti32", [128, 4], I32)
    tf32 = p.sb("tf32", [128, 4], F32)
    evb = [p.sb(f"evb{i}", [128, 512], F32) for i in range(3)]

    nc = p.start()
    p.make_ctrs()
    vc, sc, tc = p.vc, p.sc, p.tc
    ld = p.sem("ld")
    gld = p.sem("gld")
    st = p.sem("st")

    gld.inc(nc.gpsimd.dma_start(out=identb[:, :], in_=ident_in), 16)
    ld.inc(nc.sync.dma_start(out=gnwb[:, :], in_=gnw_in), 16)
    gld_id, ld_gnw = gld.n, ld.n

    evtok = [None, None, None]
    evi = [0]

    def evac_store(bk, dst_ap, ncols=512):
        j = evi[0] % 3
        evi[0] += 1
        bk.consume_wait(nc.scalar)
        if evtok[j] is not None:
            nc.scalar.wait_ge(st.h, evtok[j])
        cpi = nc.scalar.copy(evb[j][:, 0:ncols], bk.ap[:, 0:ncols])
        n = sc.inc(cpi)
        bk.release_tok((sc.h, n))
        nc.scalar.wait_ge(sc.h, n)
        st.inc(nc.scalar.dma_start(out=dst_ap, in_=evb[j][:, 0:ncols]), 16)
        evtok[j] = st.n

    with ExitStack() as ph1:
        e1 = ph1.enter_context
        woS = e1(nc.sbuf_tensor("woS", [128, 16, D], BF16))
        wost = e1(nc.sbuf_tensor("wost", [128, 4, D], F32))
        ot = [e1(nc.sbuf_tensor(f"ot{i}", [128, D], F32)) for i in range(2)]
        gt = [e1(nc.sbuf_tensor(f"gt{i}", [128, D], F32)) for i in range(2)]
        sqo = e1(nc.sbuf_tensor("sqo", [128, D], F32))
        sg = e1(nc.sbuf_tensor("sg", [128, D], F32))
        og = e1(nc.sbuf_tensor("og", [128, D], BF16))
        ogT = e1(nc.sbuf_tensor("ogT", [128, 16, 128], BF16))
        red4 = e1(nc.sbuf_tensor("red4", [128, 4], F32))
        pool = BankPool(p, 4, prefix="pA", ctx=ph1)
        trb = Bank(p, 4, shape=(128, 1024), dtype=BF16, name="trps", ctx=ph1)

        wo3 = wo_in.rearrange("(t p) n -> p t n", p=128)
        wo_tok = None
        for c4 in range(4):
            if wo_tok is not None:
                nc.sync.wait_ge(*wo_tok)
            ld.inc(nc.sync.dma_start(out=wost[:, :, :], in_=wo3[:, 4 * c4:4 * c4 + 4, :]), 16)
            nc.vector.wait_ge(ld.h, ld.n)
            last = None
            for j in range(4):
                et = 4 * c4 + j
                last = nc.vector.tensor_scalar(woS[:, et, :], wost[:, j, :],
                                               gnwb[:, et % 4:et % 4 + 1], None,
                                               ALU.mult)
            wo_tok = (vc.h, vc.inc(last))
        woS_tok = wo_tok

        xl_tok = [None, None]
        for it in range(8):
            bi = it % 2
            if xl_tok[bi] is not None:
                nc.sync.wait_ge(*xl_tok[bi])
            ld.inc(nc.sync.dma_start(out=ot[bi][:, :],
                                     in_=o_in[128 * it:128 * (it + 1), :]), 16)
            ld.inc(nc.sync.dma_start(out=gt[bi][:, :],
                                     in_=g_in[128 * it:128 * (it + 1), :]), 16)
            ld_t = ld.n
            nc.scalar.wait_ge(ld.h, ld_t)
            s1 = sc.inc(nc.scalar.activation(sqo[:, :], ot[bi][:, :], AF.Square))
            s2 = sc.inc(nc.scalar.activation(sg[:, :], gt[bi][:, :], AF.Silu))
            nc.vector.wait_ge(sc.h, s1)
            vstep(p, nc.vector.tensor_reduce(
                red4[:, :], sqo[:, :].rearrange("p (h e) -> p h e", h=4), AX.X,
                ALU.add))
            vstep(p, nc.vector.tensor_scalar(red4[:, :], red4[:, :], 1.0 / DV,
                                             RMSEPS, ALU.mult, ALU.add))
            emit_rsqrt(p, rstd4[:, :], red4[:, :], ti32[:, :], tf32[:, :])
            nc.vector.wait_ge(sc.h, s2)
            last = None
            for h in range(4):
                seg = slice(DV * h, DV * (h + 1))
                last = nc.vector.scalar_tensor_tensor(
                    og[:, seg], ot[bi][:, seg], rstd4[:, h:h + 1], sg[:, seg],
                    ALU.mult, ALU.mult)
            og_tok = (vc.h, vc.inc(last))
            xl_tok[bi] = og_tok
            nc.tensor.wait_ge(gld.h, gld_id)
            ogT_toks = []
            for half in range(2):
                nc.tensor.wait_ge(*og_tok)
                trb.acquire(nc.tensor)
                lastt = None
                for j in range(8):
                    dt_i = 8 * half + j
                    lastt = nc.tensor.transpose(
                        trb.ap[:, 128 * j:128 * (j + 1)],
                        og[:, 128 * dt_i:128 * (dt_i + 1)], identb[:, :])
                trb.produced(p, lastt)
                trb.consume_wait(nc.vector)
                cp = nc.vector.tensor_copy(
                    ogT[:, 8 * half:8 * half + 8, :],
                    trb.ap[:, :].rearrange("p (t n) -> p t n", t=8))
                n = vc.inc(cp)
                trb.release_tok((vc.h, n))
            ogT_tok = (vc.h, vc.n)
            nc.tensor.wait_ge(*ogT_tok)
            nc.tensor.wait_ge(*woS_tok)
            for nck in range(4):
                bk = pool.next()
                bk.acquire(nc.tensor)
                lastm = None
                for kt in range(16):
                    lastm = nc.tensor.matmul(
                        bk.ap[:, :], ogT[:, kt, :],
                        woS[:, kt, 512 * nck:512 * (nck + 1)],
                        start=(kt == 0), stop=(kt == 15))
                bk.produced(p, lastm)
                evac_store(bk, s0_scr[128 * it:128 * (it + 1),
                                      512 * nck:512 * (nck + 1)])
        for k0 in range(0, 16, 4):
            gld.inc(nc.gpsimd.dma_start(
                out=xnT[:, k0:k0 + 4, :],
                in_=xnT_in.rearrange("t p n -> p t n")[:, k0:k0 + 4, :]), 16)
        gld_xnT = gld.n
        b_tc = (tc.h, tc.n)
        b_vc = (vc.h, vc.n)
        b_sc = (sc.h, sc.n)
        b_st = st.n

    with ExitStack() as ph2:
        e2 = ph2.enter_context
        G = 256
        wgu = [e2(nc.sbuf_tensor(f"wgu{i}", [128, 16, 512], BF16)) for i in range(2)]
        hm = e2(nc.sbuf_tensor("hm", [128, 64, G], BF16))
        h2t = e2(nc.sbuf_tensor("h2t", [128, G], BF16))
        sacc = e2(nc.sbuf_tensor("sacc", [128, 2, D], F32))
        xt2 = e2(nc.sbuf_tensor("xt2", [128, D], F32))
        s0t = e2(nc.sbuf_tensor("s0t", [128, D], F32))
        g1b = e2(nc.sbuf_tensor("g1b", [128, D], F32))
        x1b = e2(nc.sbuf_tensor("x1b", [128, D], F32))
        onesb = e2(nc.sbuf_tensor("onesb", [128, 1], BF16))
        varf = e2(nc.sbuf_tensor("varf", [128, 2], F32))
        wpj = e2(nc.sbuf_tensor("wpj", [128, 8, D], BF16))
        pool2 = BankPool(p, 4, prefix="pC", ctx=ph2)
        vbk = [Bank(p, 4 + i, shape=(128, 512), name=f"var{i}", ctx=ph2)
               for i in range(2)]

        for eng in (nc.tensor, nc.vector, nc.scalar, nc.gpsimd, nc.sync):
            eng.wait_ge(*b_tc)
            eng.wait_ge(*b_vc)
            eng.wait_ge(*b_sc)
            eng.wait_ge(st.h, b_st)
        vstep(p, nc.vector.memset(onesb[:, :], 1.0))
        ones_tok = (vc.h, vc.n)
        nc.tensor.wait_ge(gld.h, gld_xnT)
        nc.tensor.wait_ge(*ones_tok)

        wgu3 = wguT_in.rearrange("(t p) n -> p t n", p=128)
        wpj3 = wpjT_in.rearrange("(t p) n -> p t n", p=128)

        group_end = None
        for g4 in range(4):
            gsl = slice(G * g4, G * (g4 + 1))
            if group_end is not None:
                for eng in (nc.tensor, nc.vector, nc.scalar, nc.gpsimd, nc.sync):
                    for t in group_end:
                        eng.wait_ge(*t)
            wld_tok = [None, None]
            lastg = None
            for mc in range(32):
                wb = wgu[mc % 2]
                if wld_tok[mc % 2] is not None:
                    nc.gpsimd.wait_ge(*wld_tok[mc % 2])
                gld.inc(nc.gpsimd.dma_start(
                    out=wb[:, :, :],
                    in_=wgu3[:, :, 512 * mc:512 * (mc + 1)]), 16)
                gld_w = gld.n
                nc.tensor.wait_ge(gld.h, gld_w)
                for mj in range(4):
                    m = 4 * mc + mj
                    bk = pool2.next()
                    bk.acquire(nc.tensor)
                    lastm = None
                    for kt in range(16):
                        lastm = nc.tensor.matmul(
                            bk.ap[:, 0:G], wb[:, kt, 128 * mj:128 * (mj + 1)],
                            xnT[:, kt, gsl], start=(kt == 0), stop=(kt == 15))
                    bk.produced(p, lastm)
                    if m < 64:
                        bk.consume_wait(nc.scalar)
                        cpi = nc.scalar.activation(hm[:, m, :], bk.ap[:, 0:G],
                                                   AF.Silu)
                        n = sc.inc(cpi)
                        bk.release_tok((sc.h, n))
                        lastg = (sc.h, n)
                    else:
                        bk.consume_wait(nc.vector)
                        if m == 64 and lastg is not None:
                            nc.vector.wait_ge(*lastg)
                        cpi = nc.vector.tensor_mul(hm[:, m - 64, :],
                                                   hm[:, m - 64, :], bk.ap[:, 0:G])
                        n = vc.inc(cpi)
                        bk.release_tok((vc.h, n))
                        nc.vector.wait_ge(vc.h, n)
                        n2 = vc.inc(nc.vector.tensor_mul(h2t[:, :],
                                                         hm[:, m - 64, :],
                                                         hm[:, m - 64, :]))
                        nc.tensor.wait_ge(vc.h, n2)
                        for tb in range(2):
                            if m == 64:
                                vbk[tb].acquire(nc.tensor)
                            mmv = nc.tensor.matmul(
                                vbk[tb].ap[:, 0:1],
                                h2t[:, 128 * tb:128 * (tb + 1)], onesb[:, :],
                                start=(m == 64), stop=(m == 127))
                            if m == 127:
                                vbk[tb].produced(p, mmv)
                wld_tok[mc % 2] = (tc.h, tc.n)
            for tb in range(2):
                vbk[tb].consume_wait(nc.vector)
                cpv = nc.vector.tensor_copy(varf[:, tb:tb + 1], vbk[tb].ap[:, 0:1])
                n = vc.inc(cpv)
                vbk[tb].release_tok((vc.h, n))
            nc.vector.wait_ge(vc.h, vc.n)
            vstep(p, nc.vector.tensor_scalar(varf[:, :], varf[:, :], 1.0 / 8192,
                                             F32EPS, ALU.mult, ALU.add))
            emit_rsqrt(p, rstd4[:, 0:2], varf[:, :], ti32[:, 0:2], tf32[:, 0:2])
            rstd_tok = (vc.h, vc.n)
            wpj_tok = None
            lastd = None
            for mc8 in range(8):
                if wpj_tok is not None:
                    nc.gpsimd.wait_ge(*wpj_tok)
                gld.inc(nc.gpsimd.dma_start(
                    out=wpj[:, :, :], in_=wpj3[:, 8 * mc8:8 * (mc8 + 1), :]), 16)
                gld_w = gld.n
                nc.tensor.wait_ge(gld.h, gld_w)
                for tb in range(2):
                    for nck in range(4):
                        bk = pool2.next()
                        bk.acquire(nc.tensor)
                        lastm = None
                        for kt in range(8):
                            m = 8 * mc8 + kt
                            lastm = nc.tensor.matmul(
                                bk.ap[:, :],
                                hm[:, m, 128 * tb:128 * (tb + 1)],
                                wpj[:, kt, 512 * nck:512 * (nck + 1)],
                                start=(kt == 0), stop=(kt == 7))
                        bk.produced(p, lastm)
                        bk.consume_wait(nc.vector)
                        dsl = slice(512 * nck, 512 * (nck + 1))
                        if mc8 == 0:
                            cpi = nc.vector.tensor_copy(sacc[:, tb, dsl],
                                                        bk.ap[:, :])
                        else:
                            cpi = nc.vector.tensor_add(sacc[:, tb, dsl],
                                                       sacc[:, tb, dsl],
                                                       bk.ap[:, :])
                        n = vc.inc(cpi)
                        bk.release_tok((vc.h, n))
                        lastd = (vc.h, n)
                wpj_tok = lastd
            sacc_tok = lastd
            for tb in range(2):
                it = 2 * g4 + tb
                if g4 + tb > 0:
                    nc.sync.wait_ge(vc.h, vc.n)
                ld.inc(nc.sync.dma_start(out=xt2[:, :],
                                         in_=x_in[128 * it:128 * (it + 1), :]), 16)
                ld.inc(nc.sync.dma_start(out=s0t[:, :],
                                         in_=s0_scr[128 * it:128 * (it + 1), :]), 16)
                ld_t = ld.n
                nc.scalar.wait_ge(ld.h, ld_t)
                nc.scalar.wait_ge(*sacc_tok)
                nc.scalar.wait_ge(*rstd_tok)
                sA = sc.inc(nc.scalar.activation(g1b[:, :], sacc[:, tb, :],
                                                 AF.Silu,
                                                 scale=rstd4[:, tb:tb + 1]))
                nc.vector.wait_ge(sc.h, sA)
                nc.vector.wait_ge(st.h, st.n)
                nc.vector.tensor_mul(g1b[:, :], g1b[:, :], s0t[:, :])
                nV = vc.inc(nc.vector.tensor_add(x1b[:, :], xt2[:, :],
                                                 g1b[:, :]))
                nc.scalar.wait_ge(vc.h, nV)
                st.inc(nc.scalar.dma_start(out=x1_out[128 * it:128 * (it + 1), :],
                                           in_=x1b[:, :]), 16)
            group_end = [(tc.h, tc.n), (vc.h, vc.n), (sc.h, sc.n)]

    nc.scalar.wait_ge(st.h, st.n)
    nc.vector.wait_ge(st.h, st.n)
    p.finish()
    return p.nc


def run_ffn0(inputs, x_np, l0a_res, o_full, qres):
    nc = build_ffn0()
    gnw = cc(np.asarray(inputs["L0_attn_g_norm_weight"]).reshape(4, 128).T)
    o_tok = o_full.transpose(0, 2, 1, 3).reshape(B, S, D)
    wguT = np.ascontiguousarray(qres["wguT"])
    wpjT = np.ascontiguousarray(qres["wpjT"])
    in_maps = []
    for c in range(NCORE):
        b, hh = c // 2, c % 2
        sl = slice(TC * hh, TC * (hh + 1))
        m = dict(
            x=cc(x_np[b, sl]), o=cc(o_tok[b, sl]), g=cc(l0a_res[c]["g"]),
            xnT=l0a_res[c]["xnT"], gnw=gnw, wo=cc(inputs["L0_attn_Wo"]),
            wguT=wguT, wpjT=wpjT, ident=IDENT,
        )
        in_maps.append(m)
    res = run_bass_kernel_spmd(nc, in_maps, core_ids=list(range(NCORE)))
    return [r["x1"] for r in res.results]



def build_l1attn():
    p = Prog("l1attn")
    nc = p.nc
    x1_in = p.din("x1", [S, D])
    twc_in = p.din("twc", [D, 1024], BF16)
    twr_in = p.din("twr", [1024, D], BF16)
    wvT_in = p.din("wvT", [D, 1024], BF16)
    wg1T_in = p.din("wg1T", [D, 512], BF16)
    wg2T_in = p.din("wg2T", [512, 1024], BF16)
    bu_in = p.din("bu", [128, 8])
    bv_in = p.din("bv", [128, 8])
    bg1_in = p.din("bg1", [128, 4])
    bg2_in = p.din("bg2", [128, 8])
    cw_in = p.din("cw", [128, 16, 3])
    cbc_in = p.din("cbc", [128, 16])
    ident_in = p.din("ident", [128, 128])

    uvT_out = p.dout("uvT", [16, 128, S])
    toutT_out = p.dout("toutT", [16, 128, S])
    xn1T_out = p.dout("xn1T", [16, 128, S], BF16)

    xn1T = p.sb("s_xn1T", [128, 16, S + 2], BF16)
    hT = p.sb("s_hT", [128, 4, S], BF16)
    goT = p.sb("s_goT", [128, 8, S], BF16)
    identb = p.sb("identb", [128, 128], BF16)
    biases = p.sb("biases", [128, 28], F32)
    cwb = p.sb("cwb", [128, 16, 3], F32)
    cbb = p.sb("cbb", [128, 16], F32)
    stats = p.sb("stats", [128, 1], F32)
    sqs = p.sb("sqs", [128, 1], F32)
    mcol = p.sb("mcol", [128, 1], F32)
    vcol = p.sb("vcol", [128, 1], F32)
    rstd = p.sb("rstd", [128, 1], F32)
    ti32 = p.sb("ti32", [128, 1], I32)
    tf32 = p.sb("tf32", [128, 1], F32)
    evb = [p.sb(f"evb{i}", [128, 512], F32) for i in range(3)]

    nc = p.start()
    p.make_ctrs()
    vc, sc, tc = p.vc, p.sc, p.tc
    ld = p.sem("ld")
    gld = p.sem("gld")
    st = p.sem("st")

    gld.inc(nc.gpsimd.dma_start(out=identb[:, :], in_=ident_in), 16)
    ld.inc(nc.sync.dma_start(out=biases[:, 0:8], in_=bu_in), 16)
    ld.inc(nc.sync.dma_start(out=biases[:, 8:16], in_=bv_in), 16)
    ld.inc(nc.sync.dma_start(out=biases[:, 16:24], in_=bg2_in), 16)
    ld.inc(nc.sync.dma_start(out=biases[:, 24:28], in_=bg1_in), 16)
    ld.inc(nc.sync.dma_start(out=cwb[:, :, :], in_=cw_in), 16)
    ld.inc(nc.sync.dma_start(out=cbb[:, :], in_=cbc_in), 16)
    gld_id = gld.n

    evtok = [None, None, None]
    evi = [0]

    def evac_store(bk, dst_ap, ncols=512):
        j = evi[0] % 3
        evi[0] += 1
        bk.consume_wait(nc.scalar)
        if evtok[j] is not None:
            nc.scalar.wait_ge(st.h, evtok[j])
        cpi = nc.scalar.copy(evb[j][:, 0:ncols], bk.ap[:, 0:ncols])
        n = sc.inc(cpi)
        bk.release_tok((sc.h, n))
        nc.scalar.wait_ge(sc.h, n)
        st.inc(nc.scalar.dma_start(out=dst_ap, in_=evb[j][:, 0:ncols]), 16)
        evtok[j] = st.n

    with ExitStack() as ph0:
        e0 = ph0.enter_context
        xt = [e0(nc.sbuf_tensor(f"xt{i}", [128, D], F32)) for i in range(2)]
        xn = [e0(nc.sbuf_tensor(f"xn{i}", [128, D], BF16)) for i in range(2)]
        lnscr = e0(nc.sbuf_tensor("lnscr", [128, D], F32))
        trb = Bank(p, 0, shape=(128, 1024), dtype=BF16, name="trps", ctx=ph0)
        zi = nc.vector.memset(xn1T[:, :, 0:1], 0.0)
        zi = nc.vector.memset(xn1T[:, :, S + 1:S + 2], 0.0)
        vc.inc(zi)
        ln_v = [None] * 16
        ln_s = [None] * 16
        tr_done = {}
        for it in range(16):
            bi = it % 2
            if it >= 2:
                nc.sync.wait_ge(*ln_v[it - 2])
                nc.sync.wait_ge(*ln_s[it - 2])
            ld.inc(nc.sync.dma_start(out=xt[bi][:, :],
                                     in_=x1_in[128 * it:128 * (it + 1), :]), 16)
            nc.vector.wait_ge(ld.h, ld.n)
            vstep(p, nc.vector.reduce_sum(stats[:, :], xt[bi][:, :], AX.X))
            nc.scalar.wait_ge(ld.h, ld.n)
            si = nc.scalar.activation(lnscr[:, :], xt[bi][:, :], AF.Square,
                                      accum_out=sqs[:, :])
            ln_s[it] = (sc.h, sc.inc(si))
            nc.vector.wait_ge(*ln_s[it])
            vstep(p, nc.vector.tensor_scalar(mcol[:, :], stats[:, :], 1.0 / D,
                                             None, ALU.mult))
            vstep(p, nc.vector.tensor_mul(tf32[:, :], mcol[:, :], mcol[:, :]))
            vstep(p, nc.vector.tensor_scalar(vcol[:, :], sqs[:, :], 1.0 / D,
                                             None, ALU.mult))
            vstep(p, nc.vector.tensor_sub(vcol[:, :], vcol[:, :], tf32[:, :]))
            vstep(p, nc.vector.tensor_scalar(vcol[:, :], vcol[:, :], LNEPS,
                                             None, ALU.add))
            emit_rsqrt(p, rstd[:, :], vcol[:, :], ti32[:, :], tf32[:, :])
            if it >= 2:
                nc.vector.wait_ge(*tr_done[(it - 2, 1)])
            wi = nc.vector.tensor_scalar(xn[bi][:, :], xt[bi][:, :],
                                         mcol[:, :], rstd[:, :],
                                         ALU.subtract, ALU.mult)
            ln_v[it] = (vc.h, vc.inc(wi))
            for half in range(2):
                nc.tensor.wait_ge(gld.h, gld_id)
                nc.tensor.wait_ge(*ln_v[it])
                trb.acquire(nc.tensor)
                last = None
                for j in range(8):
                    dt_i = 8 * half + j
                    last = nc.tensor.transpose(
                        trb.ap[:, 128 * j:128 * (j + 1)],
                        xn[bi][:, 128 * dt_i:128 * (dt_i + 1)], identb[:, :])
                trb.produced(p, last)
                tr_done[(it, half)] = trb.prod
                trb.consume_wait(nc.vector)
                cp = nc.vector.tensor_copy(
                    xn1T[:, 8 * half:8 * half + 8, 1 + 128 * it:1 + 128 * (it + 1)],
                    trb.ap[:, :].rearrange("p (t n) -> p t n", t=8))
                t = (vc.h, vc.inc(cp))
                trb.release_tok(t)
        xn1T_tok = (vc.h, vc.n)
        nc.scalar.wait_ge(*xn1T_tok)
        xo = xn1T_out.rearrange("t p n -> p t n")
        for k0 in range(0, 16, 4):
            st.inc(nc.scalar.dma_start(out=xo[:, k0:k0 + 4, :],
                                       in_=xn1T[:, k0:k0 + 4, 1:S + 1]), 16)
        b0_t = (tc.h, tc.n)
        b0_v = (vc.h, vc.n)
        b0_s = (sc.h, sc.n)

    with ExitStack() as ph1:
        e1 = ph1.enter_context
        wg1b = e1(nc.sbuf_tensor("wg1b", [128, 16, 512], BF16))
        wstr = [e1(nc.sbuf_tensor(f"wstr{i}", [128, 16, 256], BF16)) for i in range(2)]
        wg2b = e1(nc.sbuf_tensor("wg2b", [128, 4, 1024], BF16))
        ub = e1(nc.sbuf_tensor("ub", [128, D], F32))
        vgb = e1(nc.sbuf_tensor("vgb", [128, D], F32))
        gtb = e1(nc.sbuf_tensor("gtb", [128, D], F32))
        ksb = e1(nc.sbuf_tensor("ksb", [128, D], F32))
        onest = e1(nc.sbuf_tensor("onest", [128, D], F32))
        pool = BankPool(p, 4, prefix="p1", ctx=ph1)
        for eng in (nc.tensor, nc.vector, nc.scalar, nc.gpsimd, nc.sync):
            eng.wait_ge(*b0_t)
            eng.wait_ge(*b0_v)
            eng.wait_ge(*b0_s)
        vstep(p, nc.vector.memset(onest[:, :], 1.0))
        gld.inc(nc.gpsimd.dma_start(
            out=wg1b[:, :, :],
            in_=wg1T_in.rearrange("(t p) n -> p t n", p=128)), 16)
        gld.inc(nc.gpsimd.dma_start(
            out=wg2b[:, :, :],
            in_=wg2T_in.rearrange("(t p) n -> p t n", p=128)), 16)
        gld_w = gld.n
        nc.tensor.wait_ge(gld.h, gld_w)
        for mh in range(4):
            for ncn in range(4):
                bk = pool.next()
                bk.acquire(nc.tensor)
                last = None
                for kt in range(16):
                    last = nc.tensor.matmul(
                        bk.ap[:, :], wg1b[:, kt, 128 * mh:128 * (mh + 1)],
                        xn1T[:, kt, 1 + 512 * ncn:1 + 512 * (ncn + 1)],
                        start=(kt == 0), stop=(kt == 15))
                bk.produced(p, last)
                bk.consume_wait(nc.vector)
                cpi = nc.vector.tensor_scalar(gtb[:, 0:512], bk.ap[:, :],
                                              biases[:, 24 + mh:25 + mh], None,
                                              ALU.add)
                n = vc.inc(cpi)
                bk.release_tok((vc.h, n))
                nc.vector.wait_ge(vc.h, n)
                rl = nc.vector.tensor_relu(hT[:, mh, 512 * ncn:512 * (ncn + 1)],
                                           gtb[:, 0:512])
                vc.inc(rl)
        hT_tok = (vc.h, vc.n)
        nc.tensor.wait_ge(*hT_tok)
        wld_tok = [None, None]
        for m in range(8):
            wb = wstr[m % 2]
            if wld_tok[m % 2] is not None:
                nc.gpsimd.wait_ge(*wld_tok[m % 2])
            gld.inc(nc.gpsimd.dma_start(
                out=wb[:, :, 0:128],
                in_=twc_in.rearrange("(t p) n -> p t n", p=128)[:, :, 128 * m:128 * (m + 1)]), 16)
            gld.inc(nc.gpsimd.dma_start(
                out=wb[:, :, 128:256],
                in_=wvT_in.rearrange("(t p) n -> p t n", p=128)[:, :, 128 * m:128 * (m + 1)]), 16)
            gld_w = gld.n
            nc.tensor.wait_ge(gld.h, gld_w)
            for ncn in range(4):
                bk = pool.next()
                bk.acquire(nc.tensor)
                last = None
                for kt in range(16):
                    last = nc.tensor.matmul(
                        bk.ap[:, :], wb[:, kt, 0:128],
                        xn1T[:, kt, 1 + 512 * ncn:1 + 512 * (ncn + 1)],
                        start=(kt == 0), stop=(kt == 15))
                bk.produced(p, last)
                bk.consume_wait(nc.vector)
                cpi = nc.vector.tensor_scalar(ub[:, 512 * ncn:512 * (ncn + 1)],
                                              bk.ap[:, :], biases[:, m:m + 1],
                                              None, ALU.add)
                n = vc.inc(cpi)
                bk.release_tok((vc.h, n))
            if m > 0:
                nc.scalar.wait_ge(vc.h, prev_go_n)
            for ncn in range(4):
                bk = pool.next()
                bk.acquire(nc.tensor)
                last = None
                for kt in range(4):
                    last = nc.tensor.matmul(
                        bk.ap[:, :], wg2b[:, kt, 128 * m:128 * (m + 1)],
                        hT[:, kt, 512 * ncn:512 * (ncn + 1)],
                        start=(kt == 0), stop=(kt == 3))
                bk.produced(p, last)
                bk.consume_wait(nc.scalar)
                cpi = nc.scalar.activation(gtb[:, 512 * ncn:512 * (ncn + 1)],
                                           bk.ap[:, :], AF.Sigmoid,
                                           bias=biases[:, 16 + m:17 + m])
                n = sc.inc(cpi)
                bk.release_tok((sc.h, n))
            gates_tok = (sc.h, sc.n)
            u_tok = (vc.h, vc.n)
            for ncn in range(4):
                bk = pool.next()
                bk.acquire(nc.tensor)
                last = None
                for kt in range(16):
                    last = nc.tensor.matmul(
                        bk.ap[:, :], wb[:, kt, 128:256],
                        xn1T[:, kt, 1 + 512 * ncn:1 + 512 * (ncn + 1)],
                        start=(kt == 0), stop=(kt == 15))
                bk.produced(p, last)
                bk.consume_wait(nc.vector)
                cpi = nc.vector.tensor_scalar(vgb[:, 512 * ncn:512 * (ncn + 1)],
                                              bk.ap[:, :], biases[:, 8 + m:9 + m],
                                              None, ALU.add)
                n = vc.inc(cpi)
                bk.release_tok((vc.h, n))
            v_tok = (vc.h, vc.n)
            nc.vector.wait_ge(*gates_tok)
            vstep(p, nc.vector.tensor_mul(ub[:, :], ub[:, :], gtb[:, :]))
            vstep(p, nc.vector.tensor_mul(vgb[:, :], vgb[:, :], gtb[:, :]))
            vstep(p, nc.vector.tensor_tensor_scan(ksb[:, :], onest[:, :],
                                                  ub[:, :], 0.0, ALU.mult,
                                                  ALU.add))
            gi = nc.vector.tensor_mul(goT[:, m, :], ksb[:, :], vgb[:, :])
            go_tok = (vc.h, vc.inc(gi))
            prev_go_n = go_tok[1]
            wld_tok[m % 2] = (tc.h, tc.n)
        go_all = (vc.h, vc.n)
        b1_t = (tc.h, tc.n)
        b1_s = (sc.h, sc.n)

    with ExitStack() as ph2:
        e2 = ph2.enter_context
        twrb = e2(nc.sbuf_tensor("twrb", [128, 8, D], BF16))
        tob = e2(nc.sbuf_tensor("tob", [128, D], F32))
        tob2 = e2(nc.sbuf_tensor("tob2", [128, D], F32))
        pool2 = BankPool(p, 4, prefix="p2", ctx=ph2)
        for eng in (nc.tensor, nc.vector, nc.scalar, nc.gpsimd, nc.sync):
            eng.wait_ge(*go_all)
            eng.wait_ge(*b1_t)
            eng.wait_ge(*b1_s)
        gld.inc(nc.gpsimd.dma_start(
            out=twrb[:, :, :],
            in_=twr_in.rearrange("(t p) n -> p t n", p=128)), 16)
        gld_w = gld.n
        nc.tensor.wait_ge(gld.h, gld_w)
        for dt_i in range(16):
            for ncn in range(4):
                bk = pool2.next()
                bk.acquire(nc.tensor)
                last = None
                for kt in range(8):
                    last = nc.tensor.matmul(
                        bk.ap[:, :], twrb[:, kt, 128 * dt_i:128 * (dt_i + 1)],
                        goT[:, kt, 512 * ncn:512 * (ncn + 1)],
                        start=(kt == 0), stop=(kt == 7))
                bk.produced(p, last)
                evac_store(bk, uvT_out[dt_i, :, 512 * ncn:512 * (ncn + 1)])
        tob_tok = [None, None]
        for ktd in range(16):
            bj = ktd % 2
            dst = tob if bj == 0 else tob2
            if tob_tok[bj] is not None:
                nc.vector.wait_ge(st.h, tob_tok[bj])
            nc.vector.tensor_scalar(dst[:, :], xn1T[:, ktd, 0:S],
                                    cwb[:, ktd, 0:1], None, ALU.mult)
            nc.vector.scalar_tensor_tensor(dst[:, :], xn1T[:, ktd, 1:S + 1],
                                           cwb[:, ktd, 1:2], dst[:, :],
                                           ALU.mult, ALU.add)
            nc.vector.scalar_tensor_tensor(dst[:, :], xn1T[:, ktd, 2:S + 2],
                                           cwb[:, ktd, 2:3], dst[:, :],
                                           ALU.mult, ALU.add)
            cvi = nc.vector.tensor_scalar(dst[:, :], dst[:, :],
                                          cbb[:, ktd:ktd + 1], None, ALU.add)
            n = vc.inc(cvi)
            nc.scalar.wait_ge(vc.h, n)
            st.inc(nc.scalar.dma_start(out=toutT_out[ktd, :, :], in_=dst[:, :]), 16)
            tob_tok[bj] = st.n

    nc.scalar.wait_ge(st.h, st.n)
    p.finish()
    return p.nc


def run_l1attn(inputs, x1_full, qres):
    nc = build_l1attn()
    Tw = np.ascontiguousarray(qres["wuT"])
    WvT = np.ascontiguousarray(qres["wvT"])
    Wg1T = np.ascontiguousarray(qres["wg1T"])
    Wg2T = np.ascontiguousarray(qres["wg2T"])
    bu = np.asarray(inputs["L1_attn_bu"]).reshape(16, 128).T
    bv = np.asarray(inputs["L1_attn_bv"]).reshape(16, 128).T
    bg1 = np.asarray(inputs["L1_attn_bg1"]).reshape(4, 128).T
    bg2 = np.asarray(inputs["L1_attn_bg2"]).reshape(16, 128).T
    cw = np.asarray(inputs["L1_attn_conv_w"])[:, 0, :].reshape(16, 128, 3).transpose(1, 0, 2)
    cb = np.asarray(inputs["L1_attn_conv_b"]).reshape(16, 128).T
    in_maps = []
    for c in range(NCORE):
        b, rh = c // 2, c % 2
        rsl = slice(1024 * rh, 1024 * (rh + 1))
        m = dict(
            x1=cc(x1_full[b]),
            twc=np.ascontiguousarray(Tw[:, rsl]),
            twr=np.ascontiguousarray(Tw[rsl, :]),
            wvT=np.ascontiguousarray(WvT[:, rsl]),
            wg1T=Wg1T,
            wg2T=np.ascontiguousarray(Wg2T[:, rsl]),
            bu=cc(bu[:, 8 * rh:8 * rh + 8]), bv=cc(bv[:, 8 * rh:8 * rh + 8]),
            bg1=cc(bg1), bg2=cc(bg2[:, 8 * rh:8 * rh + 8]),
            cw=cc(cw), cbc=cc(cb), ident=IDENT,
        )
        in_maps.append(m)
    res = run_bass_kernel_spmd(nc, in_maps, core_ids=list(range(NCORE)))
    return res.results



def build_ffn1(debug=False):
    p = Prog("ffn1")
    nc = p.nc
    x1_in = p.din("x1", [TC, D])
    xn1T_in = p.din("xn1T", [16, 128, TC], BF16)
    uva_in = p.din("uva", [TC, D])
    uvb_in = p.din("uvb", [TC, D])
    tout_in = p.din("tout", [TC, D])
    wrT_in = p.din("wrT", [D, 8], BF16)
    wsgT_in = p.din("wsgT", [D, 1], BF16)
    wsgu_in = p.din("wsgu", [D, 11264])
    wsdn_in = p.din("wsdn", [5632, D])
    wegu_in = p.din("wegu", [8, D, 2816])
    wedn_in = p.din("wedn", [8, 1408, D])
    x2_out = p.dout("x2", [TC, D])
    if debug:
        dbg_comb = p.dout("dbg_comb", [128, 8, 8])
        dbg_sgt = p.dout("dbg_sgt", [128, 8])
        dbg_sh = p.dout("dbg_sh", [512, D])
        dbg_r = p.dout("dbg_r", [512, D])

    xn1T = p.sb("s_xn1T", [128, 16, TC], BF16)
    wrb = p.sb("wrb", [128, 16, 8], BF16)
    wsgb = p.sb("wsgb", [128, 16, 1], BF16)
    comb = p.sb("comb", [128, 8, 8], F32)
    sgt = p.sb("sgt", [128, 8], F32)
    r8a = p.sb("r8a", [128, 8], F32)
    r8b = p.sb("r8b", [128, 8], F32)
    r1a = p.sb("r1a", [128, 1], F32)
    r1b = p.sb("r1b", [128, 1], F32)
    ti32 = p.sb("ti32", [128, 1], I32)
    tf32 = p.sb("tf32", [128, 1], F32)

    nc = p.start()
    p.make_ctrs()
    vc, sc, tc = p.vc, p.sc, p.tc
    ld = p.sem("ld")
    gld = p.sem("gld")
    st = p.sem("st")

    gld.inc(nc.gpsimd.dma_start(out=wrb[:, :, :],
                                in_=wrT_in.rearrange("(t p) n -> p t n", p=128)), 16)
    gld.inc(nc.gpsimd.dma_start(out=wsgb[:, :, :],
                                in_=wsgT_in.rearrange("(t p) n -> p t n", p=128),
                                allow_slow_non_contiguous=True), 16)
    for k0 in range(0, 16, 4):
        gld.inc(nc.gpsimd.dma_start(
            out=xn1T[:, k0:k0 + 4, :],
            in_=xn1T_in.rearrange("t p n -> p t n")[:, k0:k0 + 4, :]), 16)
    gld_pre = gld.n

    with ExitStack() as ph0:
        pool0 = BankPool(p, 2, prefix="r", ctx=ph0)
        nc.tensor.wait_ge(gld.h, gld_pre)
        for tt in range(8):
            tsl = slice(128 * tt, 128 * (tt + 1))
            bk = pool0.next()
            bk.acquire(nc.tensor)
            last = None
            for kt in range(16):
                last = nc.tensor.matmul(bk.ap[:, 0:8], xn1T[:, kt, tsl],
                                        wrb[:, kt, :],
                                        start=(kt == 0), stop=(kt == 15))
            bk.produced(p, last)
            bk.consume_wait(nc.vector)
            cpi = nc.vector.tensor_copy(r8a[:, :], bk.ap[:, 0:8])
            n = vc.inc(cpi)
            bk.release_tok((vc.h, n))
            nc.vector.wait_ge(vc.h, n)
            vstep(p, nc.vector.reduce_max(r1a[:, :], r8a[:, :], AX.X,
                                          negate=True))
            nc.scalar.wait_ge(vc.h, vc.n)
            ns = sc.inc(nc.scalar.activation(r8a[:, :], r8a[:, :], AF.Exp,
                                             bias=r1a[:, :]))
            nc.vector.wait_ge(sc.h, ns)
            vstep(p, nc.vector.reduce_sum(r1a[:, :], r8a[:, :], AX.X))
            emit_recip(p, r1b[:, :], r1a[:, :], ti32[:, :], tf32[:, :])
            vstep(p, nc.vector.tensor_scalar(r8a[:, :], r8a[:, :], r1b[:, :],
                                             None, ALU.mult))
            vstep(p, nc.vector.reduce_max(r1a[:, :], r8a[:, :], AX.X))
            vstep(p, nc.vector.tensor_scalar(r8b[:, :], r8a[:, :], r1a[:, :],
                                             None, ALU.is_equal))
            vstep(p, nc.vector.scalar_tensor_tensor(r8b[:, :], r8b[:, :], -2.0,
                                                    r8a[:, :], ALU.mult,
                                                    ALU.add))
            vstep(p, nc.vector.reduce_max(r1b[:, :], r8b[:, :], AX.X))
            vstep(p, nc.vector.tensor_scalar(r8b[:, :], r8b[:, :], r1b[:, :],
                                             None, ALU.is_equal))
            vstep(p, nc.vector.tensor_scalar(r8a[:, :], r8a[:, :], r1a[:, :],
                                             None, ALU.is_equal))
            vstep(p, nc.vector.tensor_scalar(r8a[:, :], r8a[:, :], r1a[:, :],
                                             None, ALU.mult))
            vstep(p, nc.vector.tensor_scalar(r8b[:, :], r8b[:, :], r1b[:, :],
                                             None, ALU.mult))
            vstep(p, nc.vector.tensor_add(comb[:, tt, :], r8a[:, :], r8b[:, :]))
            bk = pool0.next()
            bk.acquire(nc.tensor)
            last = None
            for kt in range(16):
                last = nc.tensor.matmul(bk.ap[:, 0:1], xn1T[:, kt, tsl],
                                        wsgb[:, kt, :],
                                        start=(kt == 0), stop=(kt == 15))
            bk.produced(p, last)
            bk.consume_wait(nc.scalar)
            cpi = nc.scalar.activation(sgt[:, tt:tt + 1], bk.ap[:, 0:1],
                                       AF.Sigmoid)
            n = sc.inc(cpi)
            bk.release_tok((sc.h, n))
            nc.scalar.wait_ge(sc.h, n)
        rt_t = (tc.h, tc.n)
        rt_v = (vc.h, vc.n)
        rt_s = (sc.h, sc.n)
        if debug:
            nc.scalar.wait_ge(*rt_v)
            nc.scalar.wait_ge(*rt_s)
            st.inc(nc.scalar.dma_start(out=dbg_comb, in_=comb[:, :, :]), 16)
            st.inc(nc.scalar.dma_start(out=dbg_sgt, in_=sgt[:, :]), 16)

    with ExitStack() as ph1:
        e1 = ph1.enter_context
        G = 512
        hs = e1(nc.sbuf_tensor("hs", [128, 44, G], BF16))
        he = e1(nc.sbuf_tensor("he", [128, 11, G], BF16))
        wstr = [e1(nc.sbuf_tensor(f"wstr{i}", [128, 16, 256], BF16)) for i in range(2)]
        wdn = e1(nc.sbuf_tensor("wdn", [128, 6, D], BF16))
        racc = e1(nc.sbuf_tensor("racc", [128, 4, D], F32))
        xt2 = e1(nc.sbuf_tensor("xt2", [128, D], F32))
        sv1 = e1(nc.sbuf_tensor("sv1", [128, D], F32))
        sv2 = e1(nc.sbuf_tensor("sv2", [128, D], F32))
        g1b = e1(nc.sbuf_tensor("g1b", [128, D], F32))
        pool = BankPool(p, 6, prefix="f", ctx=ph1)
        for eng in (nc.tensor, nc.vector, nc.scalar, nc.gpsimd, nc.sync):
            eng.wait_ge(*rt_t)
            eng.wait_ge(*rt_v)
            eng.wait_ge(*rt_s)

        wsgu3 = wsgu_in.rearrange("(t p) n -> p t n", p=128)
        wsdn3 = wsdn_in.rearrange("(t p) n -> p t n", p=128)

        group_end = None
        for g01 in range(2):
            gsl = slice(G * g01, G * (g01 + 1))
            if group_end is not None:
                for eng in (nc.tensor, nc.vector, nc.scalar, nc.gpsimd, nc.sync):
                    for t in group_end:
                        eng.wait_ge(*t)

            def gu_pass(src3, n_m, dst, dst_tiles, wchunks):
                wld = [None, None]
                lastg = [None]
                half = n_m // 2
                for mc in range(wchunks):
                    wb = wstr[mc % 2]
                    nm_c = min(2, n_m - 2 * mc)
                    if wld[mc % 2] is not None:
                        nc.gpsimd.wait_ge(*wld[mc % 2])
                    gld.inc(nc.gpsimd.dma_start(
                        out=wb[:, :, 0:128 * nm_c],
                        in_=src3[:, :, 256 * mc:256 * mc + 128 * nm_c]), 16)
                    nc.tensor.wait_ge(gld.h, gld.n)
                    for mj in range(nm_c):
                        m = 2 * mc + mj
                        bk = pool.next()
                        bk.acquire(nc.tensor)
                        last = None
                        for kt in range(16):
                            last = nc.tensor.matmul(
                                bk.ap[:, :], wb[:, kt, 128 * mj:128 * (mj + 1)],
                                xn1T[:, kt, gsl], start=(kt == 0), stop=(kt == 15))
                        bk.produced(p, last)
                        if m < half:
                            bk.consume_wait(nc.scalar)
                            cpi = nc.scalar.activation(dst[:, m, :], bk.ap[:, :],
                                                       AF.Silu)
                            n = sc.inc(cpi)
                            bk.release_tok((sc.h, n))
                            lastg[0] = (sc.h, n)
                        else:
                            bk.consume_wait(nc.vector)
                            if m == half and lastg[0] is not None:
                                nc.vector.wait_ge(*lastg[0])
                            cpi = nc.vector.tensor_mul(dst[:, m - half, :],
                                                       dst[:, m - half, :],
                                                       bk.ap[:, :])
                            n = vc.inc(cpi)
                            bk.release_tok((vc.h, n))
                    wld[mc % 2] = (tc.h, tc.n)
                return (vc.h, vc.n)

            h_tok = gu_pass(wsgu3, 88, hs, 44, 44)
            nc.tensor.wait_ge(*h_tok)
            wdl = None
            lastd = None
            for mc4 in range(8):
                kn = min(6, 44 - 6 * mc4)
                if wdl is not None:
                    nc.gpsimd.wait_ge(*wdl)
                gld.inc(nc.gpsimd.dma_start(
                    out=wdn[:, 0:kn, :],
                    in_=wsdn3[:, 6 * mc4:6 * mc4 + kn, :]), 16)
                nc.tensor.wait_ge(gld.h, gld.n)
                for tb in range(4):
                    for ncn in range(4):
                        bk = pool.next()
                        bk.acquire(nc.tensor)
                        last = None
                        for kt in range(kn):
                            last = nc.tensor.matmul(
                                bk.ap[:, :],
                                hs[:, 6 * mc4 + kt, 128 * tb:128 * (tb + 1)],
                                wdn[:, kt, 512 * ncn:512 * (ncn + 1)],
                                start=(kt == 0), stop=(kt == kn - 1))
                        bk.produced(p, last)
                        bk.consume_wait(nc.vector)
                        dsl = slice(512 * ncn, 512 * (ncn + 1))
                        if mc4 == 0:
                            cpi = nc.vector.tensor_copy(racc[:, tb, dsl],
                                                        bk.ap[:, :])
                        else:
                            cpi = nc.vector.tensor_add(racc[:, tb, dsl],
                                                       racc[:, tb, dsl],
                                                       bk.ap[:, :])
                        n = vc.inc(cpi)
                        bk.release_tok((vc.h, n))
                        lastd = (vc.h, n)
                wdl = lastd
            for tb in range(4):
                tt = 4 * g01 + tb
                vstep(p, nc.vector.tensor_scalar(racc[:, tb, :], racc[:, tb, :],
                                                 sgt[:, tt:tt + 1], None,
                                                 ALU.mult))
            shared_tok = (vc.h, vc.n)
            if debug and g01 == 0:
                nc.scalar.wait_ge(*shared_tok)
                for tb in range(4):
                    st.inc(nc.scalar.dma_start(
                        out=dbg_sh[128 * tb:128 * (tb + 1), :],
                        in_=racc[:, tb, :]), 16)
                nc.vector.wait_ge(st.h, st.n)

            for ex in range(8):
                we3 = wegu_in[ex].rearrange("(t p) n -> p t n", p=128)
                wd3 = wedn_in[ex].rearrange("(t p) n -> p t n", p=128)
                h_tok = gu_pass(we3, 22, he, 11, 11)
                nc.tensor.wait_ge(*h_tok)
                nc.gpsimd.wait_ge(*h_tok)
                eacc_first = [True] * 4
                for mc2 in range(2):
                    kn = min(6, 11 - 6 * mc2)
                    if mc2 > 0:
                        nc.gpsimd.wait_ge(tc.h, tc.n)
                    gld.inc(nc.gpsimd.dma_start(
                        out=wdn[:, 0:kn, :],
                        in_=wd3[:, 6 * mc2:6 * mc2 + kn, :]), 16)
                    nc.tensor.wait_ge(gld.h, gld.n)
                    for tb in range(4):
                        tt = 4 * g01 + tb
                        for ncn in range(4):
                            bk = pool.next()
                            bk.acquire(nc.tensor)
                            last = None
                            for kt in range(kn):
                                last = nc.tensor.matmul(
                                    bk.ap[:, :],
                                    he[:, 6 * mc2 + kt, 128 * tb:128 * (tb + 1)],
                                    wdn[:, kt, 512 * ncn:512 * (ncn + 1)],
                                    start=(kt == 0), stop=(kt == kn - 1))
                            bk.produced(p, last)
                            bk.consume_wait(nc.vector)
                            dsl = slice(512 * ncn, 512 * (ncn + 1))
                            cpi = nc.vector.scalar_tensor_tensor(
                                racc[:, tb, dsl], bk.ap[:, :],
                                comb[:, tt, ex:ex + 1], racc[:, tb, dsl],
                                ALU.mult, ALU.add)
                            n = vc.inc(cpi)
                            bk.release_tok((vc.h, n))

            if debug and g01 == 0:
                nc.scalar.wait_ge(vc.h, vc.n)
                for tb in range(4):
                    st.inc(nc.scalar.dma_start(
                        out=dbg_r[128 * tb:128 * (tb + 1), :],
                        in_=racc[:, tb, :]), 16)
                nc.vector.wait_ge(st.h, st.n)
            for tb in range(4):
                it = 4 * g01 + tb
                tsl = slice(128 * it, 128 * (it + 1))
                nc.sync.wait_ge(vc.h, vc.n)
                nc.sync.wait_ge(st.h, st.n)
                ld.inc(nc.sync.dma_start(out=xt2[:, :], in_=x1_in[tsl, :]), 16)
                ld.inc(nc.sync.dma_start(out=sv1[:, :], in_=uva_in[tsl, :]), 16)
                ld.inc(nc.sync.dma_start(out=sv2[:, :], in_=uvb_in[tsl, :]), 16)
                ld_t = ld.n
                nc.vector.wait_ge(ld.h, ld_t)
                nc.vector.wait_ge(st.h, st.n)
                v1 = vc.inc(nc.vector.tensor_add(sv1[:, :], sv1[:, :], sv2[:, :]))
                nc.sync.wait_ge(vc.h, v1)
                ld.inc(nc.sync.dma_start(out=sv2[:, :], in_=tout_in[tsl, :]), 16)
                nc.scalar.wait_ge(vc.h, vc.n)
                sA = sc.inc(nc.scalar.activation(g1b[:, :], racc[:, tb, :],
                                                 AF.Silu))
                nc.vector.wait_ge(sc.h, sA)
                nc.vector.wait_ge(ld.h, ld.n)
                vstep(p, nc.vector.tensor_add(sv1[:, :], sv1[:, :], sv2[:, :]))
                nc.vector.tensor_mul(g1b[:, :], g1b[:, :], sv1[:, :])
                nV = vc.inc(nc.vector.tensor_add(xt2[:, :], xt2[:, :], g1b[:, :]))
                nc.scalar.wait_ge(vc.h, nV)
                st.inc(nc.scalar.dma_start(out=x2_out[tsl, :], in_=xt2[:, :]), 16)
            group_end = [(tc.h, tc.n), (vc.h, vc.n), (sc.h, sc.n)]

    nc.scalar.wait_ge(st.h, st.n)
    p.finish()
    return p.nc


def run_ffn1(inputs, x1_full, l1res, qres):
    nc = build_ffn1()
    in_maps = []
    for c in range(NCORE):
        b, hh = c // 2, c % 2
        tsl = slice(TC * hh, TC * (hh + 1))
        uvA = l1res[2 * b]["uvT"].reshape(D, S).T
        uvB = l1res[2 * b + 1]["uvT"].reshape(D, S).T
        tout = l1res[2 * b]["toutT"].reshape(D, S).T
        m = dict(
            x1=cc(x1_full[b, tsl]),
            xn1T=np.ascontiguousarray(l1res[2 * b]["xn1T"][:, :, tsl]),
            uva=cc(uvA[tsl]), uvb=cc(uvB[tsl]), tout=cc(tout[tsl]),
            wrT=np.ascontiguousarray(qres["wrT"]),
            wsgT=np.ascontiguousarray(qres["wsgT"]),
            wsgu=cc(inputs["L1_ffn_Ws_gate_up"]),
            wsdn=cc(inputs["L1_ffn_Ws_down"]),
            wegu=cc(inputs["L1_ffn_We_gate_up"]),
            wedn=cc(inputs["L1_ffn_We_down"]),
        )
        in_maps.append(m)
    res = run_bass_kernel_spmd(nc, in_maps, core_ids=list(range(NCORE)))
    return [r["x2"] for r in res.results]


def run_l0a(inputs, x_np):
    nc = build_l0a()
    common = dict(
        wq=cc(inputs["L0_attn_Wq"]), wk=cc(inputs["L0_attn_Wk"]),
        wv=cc(inputs["L0_attn_Wv"]), wg=cc(inputs["L0_attn_Wg"]),
        wgk1=cc(inputs["L0_attn_Wgk1"]), wgk2=cc(inputs["L0_attn_Wgk2"]),
        bgk2=cc(np.asarray(inputs["L0_attn_bgk2"]).reshape(8, 128).T), ident=IDENT,
    )
    in_maps = []
    for c in range(NCORE):
        b, h = c // 2, c % 2
        m = dict(common)
        m["x"] = cc(x_np[b, TC * h:TC * (h + 1), :])
        in_maps.append(m)
    res = run_bass_kernel_spmd(nc, in_maps, core_ids=list(range(NCORE)))
    return res.results


def kernel(**inputs):
    x = np.ascontiguousarray(np.asarray(inputs["L0_attn_hidden_states"],
                                        dtype=np.float32))
    l0a = run_l0a(inputs, x)
    o_full = run_gla(l0a)
    qres = run_quant(inputs)
    x1_parts = run_ffn0(inputs, x, l0a, o_full, qres)
    x1_full = np.stack([np.concatenate([x1_parts[2 * b], x1_parts[2 * b + 1]],
                                       axis=0) for b in range(B)])
    l1res = run_l1attn(inputs, x1_full, qres)
    x2_parts = run_ffn1(inputs, x1_full, l1res, qres)
    x2 = np.stack([np.concatenate([x2_parts[2 * b], x2_parts[2 * b + 1]],
                                  axis=0) for b in range(B)])
    return x2.astype(np.float32)
